# version 01
# baseline (speedup 1.0000x reference)
import numpy as np
import ml_dtypes
from contextlib import ExitStack
import concourse.bass as bass
import concourse.mybir as mybir
from concourse.bass import IndirectOffsetOnAxis
from concourse.bass_utils import run_bass_kernel_spmd

F32 = mybir.dt.float32
BF16 = mybir.dt.bfloat16
U32 = mybir.dt.uint32
ALU = mybir.AluOpType
AF = mybir.ActivationFunctionType
AX = mybir.AxisListType

NCORE = 8
SEQ = 2048
D = 1024
NTOK = 8192
NTILE = 64
NRUN = 4
CHT = 1024
NCHUNK = NTOK // CHT
FC = 8
NH = 16
HD = 64
NE = 8
DI = 2048
CAP = 8192
NST = 3
SCALE = 1.0 / 8.0
NEGBIG = -3.0e38
OOBBIG = float(1 << 24)

DEBUG = False
MAXPH = None
ATTCUT = None


def build_program(full):
    nc = bass.Bass("TRN2", target_bir_lowering=False, debug=False)

    x_in = nc.dram_tensor("x", [NTOK, D], F32, kind="ExternalInput").ap()
    wT_in = [nc.dram_tensor(n, [D, D], F32, kind="ExternalInput").ap()
             for n in ("wqT", "wkT", "wvT", "woT")]
    w1t = nc.dram_tensor("w1t", [NE, D, DI], F32, kind="ExternalInput").ap()
    w2t = nc.dram_tensor("w2t", [NE, DI, D], F32, kind="ExternalInput").ap()
    eT_in = nc.dram_tensor("eT", [D, NE], F32, kind="ExternalInput").ap()
    bias4 = nc.dram_tensor("bias4", [128, 32], F32, kind="ExternalInput").ap()
    triu_in = nc.dram_tensor("triu", [128, 128], F32, kind="ExternalInput").ap()
    ident_in = nc.dram_tensor("ident", [128, 128], F32, kind="ExternalInput").ap()
    smat_in = nc.dram_tensor("smat", [128, FC * NH], BF16, kind="ExternalInput").ap()
    emat_in = nc.dram_tensor("emat", [16, FC * 128], BF16, kind="ExternalInput").ap()
    eidx_in = nc.dram_tensor("eidx", [128, NE], F32, kind="ExternalInput").ap()
    sel_in = nc.dram_tensor("sel", [32, 8], F32, kind="ExternalInput").ap()
    if full:
        cnts_in = nc.dram_tensor("cnts", [32, NE], F32, kind="ExternalInput").ap()
        out_t = nc.dram_tensor("out", [NTOK, D], F32, kind="ExternalOutput").ap()
    else:
        cnt_out = nc.dram_tensor("cntout", [1, NRUN * NE], F32, kind="ExternalOutput").ap()

    xnbf_d = nc.dram_tensor("xnbf_d", [NTOK, D], BF16)
    if full:
        attnfm_d = nc.dram_tensor("attnfm_d", [D, NTOK], BF16)
        arena_d = nc.dram_tensor("arena_d", [NE * CAP, D], BF16)
        ext_d = nc.dram_tensor("ext_d", [NE * CAP, 2], F32)
        outpad_d = nc.dram_tensor("outpad_d", [NTOK + 128, D], F32)
    if DEBUG:
        dbg_xn = nc.dram_tensor("dbg_xn", [NTOK, D], F32, kind="ExternalOutput").ap()
        dbg_lg = nc.dram_tensor("dbg_lg", [128, NTILE * NE], F32, kind="ExternalOutput").ap()
        dbg_m12 = nc.dram_tensor("dbg_m12", [128, NTILE * NE], F32, kind="ExternalOutput").ap()
        dbg_idx = nc.dram_tensor("dbg_idx", [128, 2 * NTILE], F32, kind="ExternalOutput").ap()
        dbg_w = nc.dram_tensor("dbg_w", [128, 2 * NTILE], F32, kind="ExternalOutput").ap()
        dbg_sm = nc.dram_tensor("dbg_sm", [128, 16], F32, kind="ExternalOutput").ap()
        dbg_d = nc.dram_tensor("dbg_d", [128, 1024], F32, kind="ExternalOutput").ap()
        if full:
            dbg_at = nc.dram_tensor("dbg_at", [D, NTOK], BF16, kind="ExternalOutput").ap()
            dbg_ext = nc.dram_tensor("dbg_ext", [NE * CAP, 2], F32, kind="ExternalOutput").ap()
            dbg_kl = nc.dram_tensor("dbg_kl", [128, NTILE * NE], F32, kind="ExternalOutput").ap()
            dbg_rb = nc.dram_tensor("dbg_rb", [128, NTILE * NE], F32, kind="ExternalOutput").ap()
            dbg_rk = nc.dram_tensor("dbg_rk", [128, NTILE * NE], F32, kind="ExternalOutput").ap()
            dbg_tabs = nc.dram_tensor("dbg_tabs", [128, 32], F32, kind="ExternalOutput").ap()

    ctx = ExitStack()
    SB = lambda name, shape, dt: ctx.enter_context(nc.sbuf_tensor(name, shape, dt))
    PS = lambda name, shape: ctx.enter_context(nc.psum_tensor(name, shape, F32))

    WPOOL = SB("WPOOL", [128, 32896], BF16)
    APOOL = SB("APOOL", [128, 43520], BF16)
    FPOOL = SB("FPOOL", [128, 3 * 1024], F32)
    APF = APOOL.bitcast(F32)

    logits_all = SB("logits_all", [128, NTILE, NE], F32)
    mask_all = SB("mask_all", [128, NTILE, NE], F32)
    first1_all = SB("first1_all", [128, NTILE, NE], F32)
    first2_all = SB("first2_all", [128, NTILE, NE], F32)
    rank_all = SB("rank_all", [128, NTILE, NE], F32)
    scrA = SB("scrA", [128, NTILE, NE], F32)
    scrB = SB("scrB", [128, NTILE, NE], F32)
    kl_all = SB("kl_all", [128, NTILE, NE], F32)
    rb_all = SB("rb_all", [128, NTILE, NE], F32)
    negbig_s = SB("negbig_s", [128, NE], F32)
    idx_u = SB("idx_u", [128, 2 * NTILE], U32)
    idx_f = SB("idx_f", [128, 2 * NTILE], F32)
    wv_all = SB("wv_all", [128, 2 * NTILE], F32)
    m12s = SB("m12s", [128, 8 * NTILE], F32)
    tabs = SB("tabs", [128, 4 * NE], F32)
    triu_s = SB("triu_s", [128, 128], F32)
    ident_s = SB("ident_s", [128, 128], F32)
    smat_s = SB("smat_s", [128, FC * NH], BF16)
    emat_s = SB("emat_s", [16, FC * 128], BF16)
    eidx_s = SB("eidx_s", [128, NE], F32)
    sel_s = SB("sel_s", [128, 8], F32)
    bias_s = SB("bias_s", [128, 32], F32)
    et_s = SB("et_s", [128, FC * NE], BF16)
    cnts_s = SB("cnts_s", [128, NE], F32)
    onesr_s = SB("onesr_s", [1, 128], F32)
    ones128_s = SB("ones128_s", [128, 1], F32)
    eps_s = SB("eps_s", [128, 1], F32)
    sc_s = SB("sc_s", [16, 4 * 512], F32)
    att_s = SB("att_s", [16, 3 * 1024], BF16)
    ext4 = SB("ext4", [128, NTILE, 4], F32)
    wext = SB("wext", [128, 24], F32)
    tok_u = SB("tok_u", [128, 12], U32)

    scrC = rank_all
    tot_row_t = SB("tot_row_t", [1, 1024], F32)
    rowrep_t = SB("rowrep_t", [1, 128], F32)
    ln_sm_t = SB("ln_sm_t", [128, 16], F32)
    tot_row = tot_row_t[:, :]
    rowrep = rowrep_t[:, :]
    ln_sm = ln_sm_t[:, :]

    PB = [PS(f"PB{i}", [128, 512]) for i in range(8)]

    extpat = APOOL.bitcast(F32)[:, 3 * 1024:4 * 1024]

    def attW(w, fcc):
        return WPOOL[:, (w * FC + fcc) * 1024:(w * FC + fcc) * 1024 + 1024]

    def w1T(fcc):
        return WPOOL[:, fcc * 2048:fcc * 2048 + 2048]

    def w2T(icc):
        return WPOOL[:, 16384 + icc * 1024:16384 + icc * 1024 + 1024]

    XN_O, Q_O, K_O, V_O, PR_O, TC_O = 0, 8192, 16384, 24704, 33024, 41344

    def ap_bf(off, n):
        return APOOL[:, off:off + n]

    def xn_fm(fcc): return ap_bf(XN_O + fcc * 1024, 1024)
    def q_fm(fcc): return ap_bf(Q_O + fcc * 1024, 1024)
    def k_fm(fcc): return ap_bf(K_O + fcc * 1040, 1026)
    def v_fm(fcc): return ap_bf(V_O + fcc * 1040, 1026)
    def pr_fm(fcc): return ap_bf(PR_O + fcc * 1024, 1024)
    def y_fm(fcc): return ap_bf(XN_O + fcc * 1024, 1024)
    def tcomb(i): return ap_bf(TC_O + i * 512, 512)
    AO_O = TC_O + 1024
    def ao_sb(i): return ap_bf(AO_O + i * 512, 512)

    def ln_x(i): return APF[:, i * 1024:i * 1024 + 1024]
    def ln_scr(): return APF[:, 2 * 1024:3 * 1024]
    def ln_xn(i): return ap_bf(6 * 1024 + i * 1024, 1024)
    def zerot(): return ap_bf(8 * 1024, 1024)

    def sc_xt(i): return ap_bf(i * 1024, 1024)

    def ffn_xa(fcc): return ap_bf(fcc * (NST * 512), NST * 512)
    def ffn_int(icc): return ap_bf(FC * NST * 512 + icc * (NST * 512), NST * 512)

    def bs_x(i): return APF[:, i * 1024:i * 1024 + 1024]
    def bs_at(i): return ap_bf(4096 + i * 1024, 1024)

    def osb(i): return FPOOL[:, i * 1024:i * 1024 + 1024]
    def bs_o(i): return FPOOL[:, 2 * 1024:3 * 1024]

    sem_names = ["bar", "sL", "sS", "gD", "gS", "pe", "dv", "ac", "po"]
    SEM = {n: ctx.enter_context(nc.semaphore(n)) for n in sem_names}
    ENGS = ["sync", "tensor", "vector", "scalar", "gpsimd"]

    BAR_ENGS = ["sync", "vector", "scalar", "gpsimd"]

    def barrier(eng, E, C):
        for k in BAR_ENGS:
            C["bar"] += 1
            if eng == k:
                E[k].sem_inc(SEM["bar"], 1)
        E[eng].wait_ge(SEM["bar"], C["bar"])

    def mk(eng, E, C):
        def dma(kind, sem, out, in_, **kw):
            C[sem] += 16
            if eng == kind:
                E[kind].dma_start(out=out, in_=in_, **kw).then_inc(SEM[sem], 16)

        def on(kind):
            return eng == kind

        def wait(kind, sem, val):
            if eng == kind:
                E[kind].wait_ge(SEM[sem], val)

        def inc(kind, sem, n=1):
            C[sem] += n
            if eng == kind:
                E[kind].sem_inc(SEM[sem], n)

        def vop(f, kind="vector", sem="dv"):
            C[sem] += 1
            if eng == kind:
                f(E[kind]).then_inc(SEM[sem], 1)
                E[kind].wait_ge(SEM[sem], C[sem])

        return dma, on, wait, inc, vop

    def ph_const(eng, E, C):
        dma, on, wait, inc, vop = mk(eng, E, C)
        g = E.get("gpsimd")
        v = E.get("vector")

        dma("sync", "sL", triu_s[:, :], triu_in)
        dma("sync", "sL", ident_s[:, :], ident_in)
        dma("sync", "sL", smat_s[:, :], smat_in)
        dma("sync", "sL", emat_s[:, :], emat_in)
        dma("sync", "sL", eidx_s[:, :], eidx_in)
        dma("sync", "sL", sel_s[0:32, :], sel_in)
        dma("sync", "sL", bias_s[:, :], bias4)
        if full:
            dma("sync", "sL", cnts_s[0:32, :], cnts_in)
        for fcc in range(FC):
            dma("gpsimd", "gD", et_s[:, fcc * NE:(fcc + 1) * NE],
                eT_in[fcc * 128:(fcc + 1) * 128, :])
        if full:
            for w in range(4):
                for fcc in range(FC):
                    dma("gpsimd", "gD", attW(w, fcc),
                        wT_in[w][fcc * 128:(fcc + 1) * 128, :])
        C["dv"] += 1
        if on("vector"):
            v.memset(sel_s[32:64, :], 0.0)
            v.memset(sel_s[64:128, :], 0.0)
            v.memset(cnts_s[32:64, :], 0.0)
            v.memset(cnts_s[64:128, :], 0.0)
            v.memset(tabs[32:64, :], 0.0)
            v.memset(tabs[64:128, :], 0.0)
            v.memset(negbig_s[:, :], NEGBIG)
            v.memset(onesr_s[:, :], 1.0)
            v.memset(ones128_s[:, :], 1.0)
            v.memset(eps_s[:, :], 1e-5)
            v.memset(zerot(), 0.0).then_inc(SEM["dv"], 1)
        if full:
            C["po"] += 1
            if on("gpsimd"):
                g.iota(extpat[:, :].rearrange("p (b a c) -> p b a c", b=4, c=2),
                       pattern=[[0, 4], [1, 128], [0, 2]], base=NTOK,
                       channel_multiplier=0,
                       allow_small_or_imprecise_dtypes=True) \
                    .then_inc(SEM["po"], 1)
            C["dv"] += 1
            if on("vector"):
                v.wait_ge(SEM["po"], C["po"])
                v.memset(extpat[:, :].rearrange("p (a c) -> p a c", c=2)[:, :, 0:1],
                         0.0).then_inc(SEM["dv"], 1)
            wait("gpsimd", "dv", C["dv"])
            extv = ext_d.ap().rearrange("r c -> (r c)")
            nblk = NE * CAP * 2 // (128 * 1024)
            for blk in range(nblk):
                dma("gpsimd", "gD",
                    extv[blk * 131072:(blk + 1) * 131072].rearrange("(p n) -> p n", p=128),
                    extpat[:, :])
            for e in range(NE):
                for b in range(NST * 512 // 128):
                    r0 = e * CAP + b * 128
                    dma("gpsimd", "gD", arena_d[r0:r0 + 128, :], zerot())
        wait("sync", "sL", C["sL"])
        wait("gpsimd", "gD", C["gD"])
        wait("vector", "dv", C["dv"])
        barrier(eng, E, C)

    def ph_ln(eng, E, C):
        dma, on, wait, inc, vop = mk(eng, E, C)
        sL0, sS0, dv0, ac0 = C["sL"], C["sS"], C["dv"], C["ac"]
        sy, v, s = E.get("sync"), E.get("vector"), E.get("scalar")
        sums = m12s[:, 0:64]
        ssq = m12s[:, 64:128]
        mr = m12s[:, 128:192]
        rstd = m12s[:, 192:256]
        sd = m12s[:, 256:320]
        var = m12s[:, 320:384]
        tsc = m12s[:, 384:448]
        for i in range(NTILE):
            b = i % 2
            C["sL"] += 16
            if on("sync"):
                if i >= 2:
                    sy.wait_ge(SEM["dv"], dv0 + (i - 2) + 1)
                    sy.wait_ge(SEM["ac"], ac0 + (i - 2) + 1)
                sy.dma_start(out=ln_x(b), in_=x_in[i * 128:(i + 1) * 128, :]) \
                    .then_inc(SEM["sL"], 16)
            C["dv"] += 1
            if on("vector"):
                v.wait_ge(SEM["sL"], sL0 + 16 * (i + 1))
                v.reduce_sum(sums[:, i:i + 1], ln_x(b), axis=AX.X) \
                    .then_inc(SEM["dv"], 1)
            C["ac"] += 1
            if on("scalar"):
                s.wait_ge(SEM["sL"], sL0 + 16 * (i + 1))
                s.activation(out=ln_scr(), in_=ln_x(b), func=AF.Square,
                             accum_out=ssq[:, i:i + 1]).then_inc(SEM["ac"], 1)
        C["ac"] += 1
        if on("scalar"):
            s.activation(out=tsc[:, 0:2], in_=ssq[:, 62:64], func=AF.Identity) \
                .then_inc(SEM["ac"], 1)
        if on("vector"):
            v.wait_ge(SEM["dv"], dv0 + NTILE)
            v.wait_ge(SEM["ac"], ac0 + NTILE + 1)
        vop(lambda v: v.tensor_mul(tsc, sums, sums))
        vop(lambda v: v.tensor_scalar(var, tsc, -1.0 / D, 0.0, ALU.mult, ALU.add))
        vop(lambda v: v.tensor_add(var, var, ssq))
        vop(lambda v: v.tensor_scalar(var, var, 1.0 / D, 0.0, ALU.mult, ALU.add))
        sqrt_wait = C["dv"]
        if on("scalar"):
            s.wait_ge(SEM["dv"], sqrt_wait)
        vop(lambda s: s.activation(out=sd, in_=var, func=AF.Sqrt,
                                   bias=eps_s[:, :]), kind="scalar", sem="ac")
        vop(lambda s: s.activation(out=tsc[:, 0:2], in_=sd[:, 62:64],
                                   func=AF.Identity), kind="scalar", sem="ac")
        if on("vector"):
            v.wait_ge(SEM["ac"], C["ac"])
        vop(lambda v: v.reciprocal(rstd, sd))
        vop(lambda v: v.tensor_mul(tsc, rstd, sums))
        vop(lambda v: v.tensor_scalar(mr, tsc, -1.0 / D, 0.0, ALU.mult, ALU.add))
        vop(lambda v: v.tensor_copy(tsc[:, 4:5], mr[:, 0:1]))
        stats_done = C["dv"]
        for i in range(NTILE):
            b = i % 2
            C["sL"] += 16
            if on("sync"):
                if i == 0:
                    sy.wait_ge(SEM["dv"], stats_done)
                if i >= 2:
                    sy.wait_ge(SEM["dv"], stats_done + (i - 2) + 1)
                sy.dma_start(out=ln_x(b), in_=x_in[i * 128:(i + 1) * 128, :]) \
                    .then_inc(SEM["sL"], 16)
            C["dv"] += 1
            if on("vector"):
                v.wait_ge(SEM["sL"], sL0 + 16 * (NTILE + i + 1))
                if i == 0:
                    v.wait_ge(SEM["dv"], stats_done)
                if i >= 2:
                    v.wait_ge(SEM["sS"], sS0 + 16 * (i - 1))
                v.tensor_scalar(ln_xn(b), ln_x(b), rstd[:, i:i + 1], mr[:, i:i + 1],
                                ALU.mult, ALU.add).then_inc(SEM["dv"], 1)
            C["sS"] += 16
            if on("sync"):
                sy.wait_ge(SEM["dv"], stats_done + i + 1)
                sy.dma_start(out=xnbf_d[i * 128:(i + 1) * 128, :], in_=ln_xn(b)) \
                    .then_inc(SEM["sS"], 16)
        wait("sync", "sS", C["sS"])
        barrier(eng, E, C)

    def ph_att(eng, E, C):
        dma, on, wait, inc, vop = mk(eng, E, C)
        v = E.get("vector")
        if on("vector"):
            for fcc in range(FC):
                v.memset(k_fm(fcc)[:, 0:2], 0.0)
                li = v.memset(v_fm(fcc)[:, 0:2], 0.0)
        C["dv"] += 1
        if on("vector"):
            li.then_inc(SEM["dv"], 1)
        marks = {"xnfree": 0, "kvfree": C["dv"], "pb6free": 0, "pb4free": 0,
                 "pb5free": 0}
        for cc in range(NCHUNK):
            emit_att_chunk(eng, E, C, cc, marks)
        wait("sync", "sS", C["sS"])
        barrier(eng, E, C)

    def emit_att_chunk(eng, E, C, cc, M):
        dma, on, wait, inc, vop = mk(eng, E, C)
        sy, t, v, s, g = (E.get(k) for k in ("sync", "tensor", "vector", "scalar", "gpsimd"))
        t0 = cc * CHT
        even = (cc % 2 == 0)

        if on("sync") and cc > 0:
            sy.wait_ge(SEM["pe"], M["xnfree"])
        for fcc in range(FC):
            dma("sync", "sL", xn_fm(fcc),
                xnbf_d[t0:t0 + CHT, fcc * 128:(fcc + 1) * 128], transpose=True)
        xn_ready = C["sL"]
        if ATTCUT == 1:
            if on("sync"):
                sy.wait_ge(SEM["sL"], xn_ready)
            M["xnfree"] = C["pe"]
            return

        ac0 = C["ac"]
        gidx = 0
        for w in range(3 if full else 0):
            for fo in range(FC):
                for nt in range(2):
                    bank = PB[gidx % 4]
                    C["pe"] += 1
                    if on("tensor"):
                        if gidx == 0:
                            t.wait_ge(SEM["sL"], xn_ready)
                        tgt = ac0 + gidx - 3
                        if tgt > 0:
                            t.wait_ge(SEM["ac"], tgt)
                        for fcc in range(FC):
                            mm = t.matmul(bank[:, :],
                                          attW(w, fcc)[:, fo * 128:(fo + 1) * 128],
                                          xn_fm(fcc)[:, nt * 512:(nt + 1) * 512],
                                          start=(fcc == 0), stop=(fcc == FC - 1))
                        mm.then_inc(SEM["pe"], 1)
                    C["ac"] += 1
                    if on("scalar"):
                        s.wait_ge(SEM["pe"], C["pe"])
                        if gidx == 0:
                            s.wait_ge(SEM["dv"], M["kvfree"])
                        dst = (q_fm(fo), k_fm(fo), v_fm(fo))[w]
                        off = nt * 512 + (0 if w == 0 else 2)
                        s.activation(out=dst[:, off:off + 512], in_=bank[:, :],
                                     func=AF.Identity,
                                     bias=bias_s[:, w * 8 + fo:w * 8 + fo + 1]) \
                            .then_inc(SEM["ac"], 1)
                    gidx += 1
        qkv_done = C["ac"]

        ev_nt0 = 0
        for nt in range(2):
            C["pe"] += 1
            if on("tensor"):
                if nt == 0:
                    t.wait_ge(SEM["sL"], xn_ready)
                    if cc > 0:
                        t.wait_ge(SEM["dv"], M["pb4free"])
                else:
                    t.wait_ge(SEM["dv"], ev_nt0)
                for fcc in range(FC):
                    mm = t.matmul(PB[4][0:NE, :], et_s[:, fcc * NE:(fcc + 1) * NE],
                                  xn_fm(fcc)[:, nt * 512:(nt + 1) * 512],
                                  start=(fcc == 0), stop=(fcc == FC - 1))
                mm.then_inc(SEM["pe"], 1)
            C["dv"] += 1
            if on("vector"):
                v.wait_ge(SEM["pe"], C["pe"])
                v.tensor_copy(sc_s[0:NE, nt * 512:(nt + 1) * 512], PB[4][0:NE, :]) \
                    .then_inc(SEM["dv"], 1)
            ev_nt0 = C["dv"]
        M["pb4free"] = C["dv"]
        lg_em = C["dv"]
        if ATTCUT == 2:
            M["xnfree"] = C["pe"]
            return
        C["pe"] += 1
        if on("tensor"):
            t.wait_ge(SEM["dv"], lg_em)
            if cc > 0:
                t.wait_ge(SEM["dv"], M["pb5free"])
            for tt in range(8):
                t.transpose(PB[5][:, tt * 8:(tt + 1) * 8],
                            sc_s[0:NE, tt * 128:(tt + 1) * 128],
                            ident_s[0:NE, 0:NE])
            t.matmul(PB[5][:, 256:384], onesr_s[:, :], onesr_s[:, :],
                     start=True, stop=True).then_inc(SEM["pe"], 1)
        C["dv"] += 1
        if on("vector"):
            v.wait_ge(SEM["pe"], C["pe"])
            v.tensor_copy(logits_all[:, cc * 8:(cc + 1) * 8, :],
                          PB[5][:, 0:64].rearrange("p (t e) -> p t e", e=NE)) \
                .then_inc(SEM["dv"], 1)
        M["pb5free"] = C["dv"]

        if not full:
            M["xnfree"] = C["pe"]
            return

        scmm = 0
        for j in range(3):
            C["dv"] += 1
            if on("vector"):
                if j == 0:
                    v.wait_ge(SEM["ac"], qkv_done)
                else:
                    v.wait_ge(SEM["pe"], scmm)
                for fcc in range(FC):
                    li = v.tensor_mul(pr_fm(fcc), q_fm(fcc),
                                      k_fm(fcc)[:, 2 - j:2 - j + 1024])
                li.then_inc(SEM["dv"], 1)
            pr_j = C["dv"]
            C["pe"] += 1
            if on("tensor"):
                t.wait_ge(SEM["dv"], pr_j)
                if j == 0 and cc > 0:
                    t.wait_ge(SEM["dv"], M["pb6free"])
                for nt in range(2):
                    dstb = PB[6] if nt == 0 else PB[7]
                    for fcc in range(FC):
                        mm = t.matmul(dstb[32 * j:32 * j + NH, :],
                                      smat_s[:, fcc * NH:(fcc + 1) * NH],
                                      pr_fm(fcc)[:, nt * 512:(nt + 1) * 512],
                                      start=(fcc == 0), stop=(fcc == FC - 1))
                mm.then_inc(SEM["pe"], 1)
            scmm = C["pe"]
        sc_mm_done = C["pe"]

        att_ready = []
        for nt in range(2):
            bank = PB[6] if nt == 0 else PB[7]
            C["dv"] += 1
            if on("vector"):
                v.wait_ge(SEM["pe"], sc_mm_done)
                for j in range(3):
                    v.tensor_copy(sc_s[0:NH, j * 512:(j + 1) * 512],
                                  bank[32 * j:32 * j + NH, :])
                if even and nt == 0:
                    v.memset(sc_s[0:NH, 512:513], NEGBIG)
                    v.memset(sc_s[0:NH, 1024:1026], NEGBIG)
                v.tensor_max(sc_s[0:NH, 1536:2048],
                             sc_s[0:NH, 0:512], sc_s[0:NH, 512:1024])
                v.tensor_max(sc_s[0:NH, 1536:2048],
                             sc_s[0:NH, 1536:2048], sc_s[0:NH, 1024:1536])
                for j in range(3):
                    li = v.tensor_sub(sc_s[0:NH, j * 512:(j + 1) * 512],
                                      sc_s[0:NH, j * 512:(j + 1) * 512],
                                      sc_s[0:NH, 1536:2048])
                li.then_inc(SEM["dv"], 1)
            C["ac"] += 1
            if on("scalar"):
                s.wait_ge(SEM["dv"], C["dv"])
                for j in range(3):
                    li = s.activation(out=sc_s[0:NH, j * 512:(j + 1) * 512],
                                      in_=sc_s[0:NH, j * 512:(j + 1) * 512],
                                      func=AF.Exp, scale=SCALE)
                li.then_inc(SEM["ac"], 1)
            C["dv"] += 1
            if on("vector"):
                v.wait_ge(SEM["ac"], C["ac"])
                v.tensor_add(sc_s[0:NH, 1536:2048],
                             sc_s[0:NH, 0:512], sc_s[0:NH, 512:1024])
                v.tensor_add(sc_s[0:NH, 1536:2048],
                             sc_s[0:NH, 1536:2048], sc_s[0:NH, 1024:1536])
                v.reciprocal(sc_s[0:NH, 1536:2048], sc_s[0:NH, 1536:2048])
                for j in range(3):
                    li = v.tensor_mul(
                        att_s[0:NH, j * 1024 + nt * 512:j * 1024 + (nt + 1) * 512],
                        sc_s[0:NH, j * 512:(j + 1) * 512],
                        sc_s[0:NH, 1536:2048])
                li.then_inc(SEM["dv"], 1)
            att_ready.append(C["dv"])
        M["pb6free"] = C["dv"]

        comb_mark = C["po"]
        for nt in range(2):
            for fcc in range(FC):
                C["pe"] += 1
                if on("tensor"):
                    if nt == 0 and fcc == 0:
                        t.wait_ge(SEM["dv"], att_ready[1])
                    if not (nt == 0 and fcc == 0):
                        t.wait_ge(SEM["dv"], C["dv"])
                    for j in range(3):
                        mm = t.matmul(PB[j][:, :], emat_s[:, fcc * 128:(fcc + 1) * 128],
                                      att_s[0:NH,
                                            j * 1024 + nt * 512:j * 1024 + (nt + 1) * 512],
                                      start=True, stop=True)
                    mm.then_inc(SEM["pe"], 1)
                C["dv"] += 1
                if on("vector"):
                    v.wait_ge(SEM["pe"], C["pe"])
                    v.tensor_mul(y_fm(fcc)[:, nt * 512:(nt + 1) * 512], PB[0][:, :],
                                 v_fm(fcc)[:, 2 + nt * 512:2 + (nt + 1) * 512])
                    v.tensor_mul(tcomb(0), PB[1][:, :],
                                 v_fm(fcc)[:, 1 + nt * 512:1 + (nt + 1) * 512])
                    v.tensor_mul(tcomb(1), PB[2][:, :],
                                 v_fm(fcc)[:, 0 + nt * 512:0 + (nt + 1) * 512]) \
                        .then_inc(SEM["dv"], 1)
                C["po"] += 1
                if on("gpsimd"):
                    g.wait_ge(SEM["dv"], C["dv"])
                    g.tensor_add(y_fm(fcc)[:, nt * 512:(nt + 1) * 512],
                                 y_fm(fcc)[:, nt * 512:(nt + 1) * 512], tcomb(0))
                    g.tensor_add(y_fm(fcc)[:, nt * 512:(nt + 1) * 512],
                                 y_fm(fcc)[:, nt * 512:(nt + 1) * 512], tcomb(1)) \
                        .then_inc(SEM["po"], 1)
                if on("vector"):
                    v.wait_ge(SEM["po"], C["po"])
        comb_done = C["po"]

        ac1 = C["ac"]
        gidx = 0
        for fo in range(FC):
            for nt in range(2):
                bank = PB[gidx % 4]
                C["pe"] += 1
                if on("tensor"):
                    if gidx == 0:
                        t.wait_ge(SEM["po"], comb_done)
                    tgt = ac1 + gidx - 3
                    if tgt > ac1:
                        t.wait_ge(SEM["ac"], tgt)
                    for fcc in range(FC):
                        mm = t.matmul(bank[:, :],
                                      attW(3, fcc)[:, fo * 128:(fo + 1) * 128],
                                      y_fm(fcc)[:, nt * 512:(nt + 1) * 512],
                                      start=(fcc == 0), stop=(fcc == FC - 1))
                    mm.then_inc(SEM["pe"], 1)
                C["ac"] += 1
                if on("scalar"):
                    s.wait_ge(SEM["pe"], C["pe"])
                    if C["_aodma"] >= 32:
                        s.wait_ge(SEM["sS"], C["_aodma"] - 16)
                    s.activation(out=ao_sb(gidx % 2), in_=bank[:, :], func=AF.Identity,
                                 bias=bias_s[:, 24 + fo:24 + fo + 1]) \
                        .then_inc(SEM["ac"], 1)
                C["sS"] += 16
                C["_aodma"] = C["sS"]
                if on("sync"):
                    sy.wait_ge(SEM["ac"], C["ac"])
                    sy.dma_start(out=attnfm_d[fo * 128:(fo + 1) * 128,
                                              t0 + nt * 512:t0 + (nt + 1) * 512],
                                 in_=ao_sb(gidx % 2)).then_inc(SEM["sS"], 16)
                gidx += 1
        M["xnfree"] = C["pe"]

        C["dv"] += 1
        if on("vector"):
            for fcc in range(FC):
                v.tensor_copy(k_fm(fcc)[:, 0:2], k_fm(fcc)[:, 1024:1026])
                li = v.tensor_copy(v_fm(fcc)[:, 0:2], v_fm(fcc)[:, 1024:1026])
            li.then_inc(SEM["dv"], 1)
        M["kvfree"] = C["dv"]

    def ph_top(eng, E, C):
        dma, on, wait, inc, vop = mk(eng, E, C)
        t, v, s, g = (E.get(k) for k in ("tensor", "vector", "scalar", "gpsimd"))

        def vb3(x):
            return x.unsqueeze(2).broadcast_to([128, NTILE, NE])

        col = lambda a, b: m12s[:, a * NTILE:(a + 1) * NTILE] if b is None else None
        m1c = m12s[:, 0:64]
        m2c = m12s[:, 64:128]
        d21 = m12s[:, 128:192]
        texp = m12s[:, 192:256]
        zc = m12s[:, 256:320]
        tokf = m12s[:, 320:384]
        e1c = m12s[:, 384:448]
        r1c = m12s[:, 448:512]

        C["dv"] += 1
        if on("vector"):
            v.reduce_max(m1c, logits_all[:, :, :], axis=AX.X).then_inc(SEM["dv"], 1)
            v.wait_ge(SEM["dv"], C["dv"])
            v.tensor_tensor(scrA[:, :, :], logits_all[:, :, :], vb3(m1c), ALU.is_equal)
            v.tensor_copy(scrB[:, :, 0:1], scrA[:, :, 0:1])
            v.tensor_add(scrB[:, :, 1:8], scrA[:, :, 1:8], scrA[:, :, 0:7])
            v.tensor_copy(scrC[:, :, 0:2], scrB[:, :, 0:2])
            v.tensor_add(scrC[:, :, 2:8], scrB[:, :, 2:8], scrB[:, :, 0:6])
            v.tensor_copy(scrB[:, :, 0:4], scrC[:, :, 0:4])
            v.tensor_add(scrB[:, :, 4:8], scrC[:, :, 4:8], scrC[:, :, 0:4])
            v.tensor_scalar(scrC[:, :, :], scrB[:, :, :], 1.0, 0.0, ALU.is_equal, ALU.add)
            v.tensor_mul(first1_all[:, :, :], scrC[:, :, :], scrA[:, :, :])
            v.scalar_tensor_tensor(scrA[:, :, :], first1_all[:, :, :], -1.0e30,
                                   logits_all[:, :, :], ALU.mult, ALU.add)
            v.reduce_max(m2c, scrA[:, :, :], axis=AX.X).then_inc(SEM["dv"], 1)
            v.wait_ge(SEM["dv"], C["dv"] + 1)
            v.tensor_tensor(scrB[:, :, :], scrA[:, :, :], vb3(m2c), ALU.is_equal)
            v.tensor_copy(scrC[:, :, 0:1], scrB[:, :, 0:1])
            v.tensor_add(scrC[:, :, 1:8], scrB[:, :, 1:8], scrB[:, :, 0:7])
            v.tensor_copy(scrA[:, :, 0:2], scrC[:, :, 0:2])
            v.tensor_add(scrA[:, :, 2:8], scrC[:, :, 2:8], scrC[:, :, 0:6])
            v.tensor_copy(scrC[:, :, 0:4], scrA[:, :, 0:4])
            v.tensor_add(scrC[:, :, 4:8], scrA[:, :, 4:8], scrA[:, :, 0:4])
            v.tensor_scalar(scrA[:, :, :], scrC[:, :, :], 1.0, 0.0, ALU.is_equal, ALU.add)
            v.tensor_mul(first2_all[:, :, :], scrA[:, :, :], scrB[:, :, :])
            v.tensor_add(mask_all[:, :, :], first1_all[:, :, :], first2_all[:, :, :])
            v.tensor_sub(d21, m2c, m1c).then_inc(SEM["dv"], 1)
        C["dv"] += 2
        C["ac"] += 1
        if on("scalar"):
            s.wait_ge(SEM["dv"], C["dv"])
            s.activation(out=texp, in_=d21, func=AF.Exp).then_inc(SEM["ac"], 1)
        C["dv"] += 1
        if on("vector"):
            v.wait_ge(SEM["ac"], C["ac"])
            v.tensor_scalar(zc, texp, 1.0, 0.0, ALU.add, ALU.add)
            v.reciprocal(zc, zc)
            v.tensor_copy(wv_all[:, 0:64], zc)
            v.tensor_mul(wv_all[:, 64:128], texp, zc).then_inc(SEM["dv"], 1)
        mask_done = C["dv"]

        C["pe"] += 1
        if on("tensor"):
            t.wait_ge(SEM["dv"], mask_done)
            for tt in range(NTILE):
                t.matmul(PB[0][0:1, 8 * tt:8 * tt + 8], ones128_s[:, :],
                         mask_all[:, tt, :], start=True, stop=True)
            t.matmul(PB[6][:, 0:128], onesr_s[:, :], onesr_s[:, :],
                     start=True, stop=True).then_inc(SEM["pe"], 1)
        C["dv"] += 1
        if on("vector"):
            v.wait_ge(SEM["pe"], C["pe"])
            v.tensor_copy(tot_row[0:1, 0:512], PB[0][0:1, 0:512]).then_inc(SEM["dv"], 1)
        tot_done = C["dv"]
        C["pe"] += 1
        if on("tensor"):
            for tt in range(NTILE):
                t.matmul(PB[1][:, 8 * tt:8 * tt + 8], triu_s[:, :],
                         mask_all[:, tt, :], start=True, stop=True)
            t.matmul(PB[6][:, 0:128], onesr_s[:, :], onesr_s[:, :],
                     start=True, stop=True).then_inc(SEM["pe"], 1)
        tri_done_pe = C["pe"]
        C["dv"] += 1
        if on("vector"):
            v.wait_ge(SEM["pe"], tri_done_pe)
            v.tensor_copy(rank_all[:, :, :],
                          PB[1][:, 0:512].rearrange("p (t e) -> p t e", e=NE)) \
                .then_inc(SEM["dv"], 1)
        tri_evac = C["dv"]

        if not full:
            C["dv"] += 1
            if on("vector"):
                v.wait_ge(SEM["dv"], tot_done)
                for r in range(NRUN):
                    seg = tot_row[0:1, r * 128:(r + 1) * 128] \
                        .rearrange("o (t e) -> o e t", e=NE)
                    li = v.reduce_sum(rowrep[0:1, r * 8:(r + 1) * 8],
                                      seg, axis=AX.X)
                li.then_inc(SEM["dv"], 1)
            if on("sync"):
                E["sync"].wait_ge(SEM["dv"], C["dv"])
            dma("sync", "sS", cnt_out, rowrep[0:1, 0:32])
            if on("sync"):
                E["sync"].wait_ge(SEM["sS"], C["sS"])
            barrier(eng, E, C)
            return

        C["pe"] += 1
        if on("tensor"):
            t.matmul(PB[2][0:32, 0:8], triu_s[:, 0:32], cnts_s[:, :],
                     start=True, stop=True)
            t.matmul(PB[6][:, 0:128], onesr_s[:, :], onesr_s[:, :],
                     start=True, stop=True).then_inc(SEM["pe"], 1)
        if on("vector"):
            v.wait_ge(SEM["pe"], C["pe"])
        vop(lambda v: v.tensor_copy(tabs[0:32, 8:16], PB[2][0:32, 0:8]))
        vop(lambda v: v.tensor_scalar(tabs[0:32, 16:24], tabs[0:32, 8:16], -1.0,
                                      float(CAP), ALU.mult, ALU.add))
        vop(lambda v: v.tensor_scalar(tabs[0:32, 0:8], tabs[0:32, 16:24], 0.0, 0.0,
                                      ALU.max, ALU.add))
        vop(lambda v: v.tensor_tensor(tabs[0:32, 24:32], tabs[0:32, 0:8],
                                      cnts_s[0:32, :], ALU.min))
        tabs_done = C["dv"]
        C["pe"] += 1
        if on("tensor"):
            t.wait_ge(SEM["dv"], tabs_done)
            for r in range(NRUN):
                t.matmul(PB[3][0:1, 8 * r:8 * r + 8], sel_s[:, r:r + 1],
                         tabs[:, 16:24], start=True, stop=True)
                t.matmul(PB[3][0:1, 32 + 8 * r:32 + 8 * r + 8],
                         sel_s[:, 4 + r:5 + r],
                         tabs[:, 24:32], start=True, stop=True)
            t.matmul(PB[6][:, 0:128], onesr_s[:, :], onesr_s[:, :],
                     start=True, stop=True).then_inc(SEM["pe"], 1)
        if on("vector"):
            v.wait_ge(SEM["pe"], C["pe"])
        vop(lambda v: v.tensor_copy(rowrep[0:1, 0:64], PB[3][0:1, 0:64]))
        vop(lambda v: v.tensor_copy(rowrep[0:1, 120:122], rowrep[0:1, 0:2]))
        row_done = C["dv"]
        C["pe"] += 1
        if on("tensor"):
            t.wait_ge(SEM["dv"], row_done)
            for r in range(NRUN):
                t.matmul(PB[4][:, 8 * r:8 * r + 8], onesr_s[:, :],
                         rowrep[0:1, 8 * r:8 * r + 8], start=True, stop=True)
                t.matmul(PB[4][:, 64 + 8 * r:64 + 8 * r + 8], onesr_s[:, :],
                         rowrep[0:1, 32 + 8 * r:32 + 8 * r + 8],
                         start=True, stop=True)
            t.matmul(PB[6][:, 0:128], onesr_s[:, :], onesr_s[:, :],
                     start=True, stop=True).then_inc(SEM["pe"], 1)
        C["dv"] += 1
        if on("vector"):
            v.wait_ge(SEM["pe"], C["pe"])
            for r in range(NRUN):
                src_k = PB[4][:, 8 * r:8 * r + 8].unsqueeze(1).broadcast_to([128, 16, NE])
                src_b = PB[4][:, 64 + 8 * r:64 + 8 * r + 8].unsqueeze(1) \
                    .broadcast_to([128, 16, NE])
                v.tensor_copy(kl_all[:, 16 * r:16 * (r + 1), :], src_k)
                li = v.tensor_copy(rb_all[:, 16 * r:16 * (r + 1), :], src_b)
            li.then_inc(SEM["dv"], 1)
        rep_done = C["dv"]

        A = tot_row[0:1, 0:512]
        Bv = tot_row[0:1, 512:1024]
        Cv = A
        C["dv"] += 1
        if on("vector"):
            v.wait_ge(SEM["dv"], tot_done)
            for r in range(NRUN):
                v.memset(Bv[0:1, r * 128:r * 128 + 8], 0.0)
                v.tensor_copy(Bv[0:1, r * 128 + 8:(r + 1) * 128],
                              A[0:1, r * 128:(r + 1) * 128 - 8])
            src, dst = Bv, Cv
            for step in (8, 16, 32, 64):
                for r in range(NRUN):
                    v.tensor_copy(dst[0:1, r * 128:r * 128 + step],
                                  src[0:1, r * 128:r * 128 + step])
                    li = v.tensor_add(dst[0:1, r * 128 + step:(r + 1) * 128],
                                      src[0:1, r * 128 + step:(r + 1) * 128],
                                      src[0:1, r * 128:(r + 1) * 128 - step])
                src, dst = dst, src
            li.then_inc(SEM["dv"], 1)
        base_done = C["dv"]
        C["pe"] += 1
        if on("tensor"):
            t.wait_ge(SEM["dv"], base_done)
            t.matmul(PB[5][:, :], onesr_s[:, :], Bv,
                     start=True, stop=True).then_inc(SEM["pe"], 1)
        C["dv"] += 1
        if on("vector"):
            v.wait_ge(SEM["pe"], C["pe"])
            v.tensor_add(rank_all[:, :, :], rank_all[:, :, :],
                         PB[5][:, 0:512].rearrange("p (t e) -> p t e", e=NE)) \
                .then_inc(SEM["dv"], 1)

        C["po"] += 1
        if on("gpsimd"):
            g.iota(tokf, pattern=[[128, 64]], base=0, channel_multiplier=1,
                   allow_small_or_imprecise_dtypes=True).then_inc(SEM["po"], 1)
        for ch in range(2):
            fall = (first1_all, first2_all)[ch]
            o = ch * 64
            C["dv"] += 4
            if on("vector"):
                if ch == 0:
                    v.wait_ge(SEM["po"], C["po"])
                eb = eidx_s[:, :].unsqueeze(1).broadcast_to([128, NTILE, NE])
                v.tensor_mul(scrA[:, :, :], fall[:, :, :], eb)
                v.reduce_sum(e1c, scrA[:, :, :], axis=AX.X).then_inc(SEM["dv"], 1)
                v.tensor_mul(scrA[:, :, :], fall[:, :, :], rank_all[:, :, :])
                v.reduce_sum(r1c, scrA[:, :, :], axis=AX.X).then_inc(SEM["dv"], 1)
                v.tensor_mul(scrA[:, :, :], fall[:, :, :], kl_all[:, :, :])
                v.reduce_sum(zc, scrA[:, :, :], axis=AX.X).then_inc(SEM["dv"], 1)
                v.tensor_mul(scrA[:, :, :], fall[:, :, :], rb_all[:, :, :])
                v.reduce_sum(d21, scrA[:, :, :], axis=AX.X).then_inc(SEM["dv"], 1)
                v.wait_ge(SEM["dv"], C["dv"])
            C["dv"] += 1
            if on("vector"):
                v.tensor_tensor(m2c, r1c, zc, ALU.is_lt)
                v.tensor_scalar(m1c, m2c, -OOBBIG, OOBBIG, ALU.mult, ALU.add)
                v.tensor_scalar(idx_f[:, o:o + 64], e1c, float(CAP), 0.0, ALU.mult, ALU.add)
                v.tensor_add(idx_f[:, o:o + 64], idx_f[:, o:o + 64], d21)
                v.tensor_add(idx_f[:, o:o + 64], idx_f[:, o:o + 64], r1c)
                v.tensor_add(idx_f[:, o:o + 64], idx_f[:, o:o + 64], m1c)
                v.tensor_copy(idx_u[:, o:o + 64], idx_f[:, o:o + 64]) \
                    .then_inc(SEM["dv"], 1)
        C["dv"] += 1
        if on("vector"):
            v.tensor_copy(ext4[:, :, 0:1], wv_all[:, 0:64].unsqueeze(2))
            v.tensor_copy(ext4[:, :, 1:2], tokf.unsqueeze(2))
            v.tensor_copy(ext4[:, :, 2:3], wv_all[:, 64:128].unsqueeze(2))
            v.tensor_copy(ext4[:, :, 3:4], tokf.unsqueeze(2)).then_inc(SEM["dv"], 1)
        C["_topdv"] = C["dv"]
        barrier(eng, E, C)

    def ph_scat(eng, E, C):
        dma, on, wait, inc, vop = mk(eng, E, C)
        sy, g = E.get("sync"), E.get("gpsimd")
        sL0, gS0 = C["sL"], C["gS"]
        if on("gpsimd"):
            breg = g.alloc_register("bnd")
            g.reg_mov(breg, NE * CAP - 1)
        for i in range(NTILE):
            b = i % 2
            C["sL"] += 16
            if on("sync"):
                if i >= 2:
                    sy.wait_ge(SEM["gS"], gS0 + 64 * (i - 1))
                sy.dma_start(out=sc_xt(b), in_=xnbf_d[i * 128:(i + 1) * 128, :]) \
                    .then_inc(SEM["sL"], 16)
            C["gS"] += 64
            if on("gpsimd"):
                if i == 0:
                    g.wait_ge(SEM["dv"], C["_topdv"])
                g.wait_ge(SEM["sL"], sL0 + 16 * (i + 1))
                for j in range(2):
                    g.indirect_dma_start(
                        out=arena_d[:, :],
                        out_offset=IndirectOffsetOnAxis(
                            idx_u[:, j * NTILE + i:j * NTILE + i + 1], 0),
                        in_=sc_xt(b), in_offset=None,
                        bounds_check=breg, oob_is_err=False,
                    ).then_inc(SEM["gS"], 16)
                    g.indirect_dma_start(
                        out=ext_d[:, :],
                        out_offset=IndirectOffsetOnAxis(
                            idx_u[:, j * NTILE + i:j * NTILE + i + 1], 0),
                        in_=ext4[:, i, 2 * j:2 * j + 2], in_offset=None,
                        bounds_check=breg, oob_is_err=False,
                    ).then_inc(SEM["gS"], 16)
        wait("gpsimd", "gS", C["gS"])
        barrier(eng, E, C)

    def ph_base(eng, E, C):
        dma, on, wait, inc, vop = mk(eng, E, C)
        sy, v = E.get("sync"), E.get("vector")
        sL0, sS0, dv0 = C["sL"], C["sS"], C["dv"]
        for i in range(NTILE):
            b = i % 2
            C["sL"] += 32
            if on("sync"):
                if i >= 2:
                    sy.wait_ge(SEM["dv"], dv0 + (i - 1))
                sy.dma_start(out=bs_x(b), in_=x_in[i * 128:(i + 1) * 128, :]) \
                    .then_inc(SEM["sL"], 16)
                sy.dma_start(out=bs_at(b), in_=attnfm_d[:, i * 128:(i + 1) * 128],
                             transpose=True).then_inc(SEM["sL"], 16)
            C["dv"] += 1
            if on("vector"):
                v.wait_ge(SEM["sL"], sL0 + 32 * (i + 1))
                if i >= 1:
                    v.wait_ge(SEM["sS"], sS0 + 16 * i)
                v.tensor_add(bs_o(0), bs_x(b), bs_at(b)).then_inc(SEM["dv"], 1)
            C["sS"] += 16
            if on("sync"):
                sy.wait_ge(SEM["dv"], dv0 + (i + 1))
                sy.dma_start(out=outpad_d[i * 128:(i + 1) * 128, :], in_=bs_o(0)) \
                    .then_inc(SEM["sS"], 16)
        wait("sync", "sS", C["sS"])
        barrier(eng, E, C)

    def ph_ffn(eng, E, C):
        dma, on, wait, inc, vop = mk(eng, E, C)
        sy, t, v, s, g = (E.get(k) for k in ("sync", "tensor", "vector", "scalar", "gpsimd"))
        marks = {"w1free": 0, "w2free": 0, "xafree": 0, "l2evac": 0, "scdone": 0}
        for e in range(NE):
            prev = dict(marks)
            if on("gpsimd") and e > 0:
                g.wait_ge(SEM["pe"], prev["w2free"])
            for fcc in range(FC):
                dma("gpsimd", "gD", w1T(fcc), w1t[e, fcc * 128:(fcc + 1) * 128, :])
            for icc in range(16):
                dma("gpsimd", "gD", w2T(icc), w2t[e, icc * 128:(icc + 1) * 128, :])
            w_ready = C["gD"]
            if on("sync") and e > 0:
                sy.wait_ge(SEM["pe"], prev["xafree"])
                sy.wait_ge(SEM["dv"], prev["l2evac"])
            for st in range(NST):
                for fcc in range(FC):
                    dma("sync", "sL", ffn_xa(fcc)[:, st * 512:(st + 1) * 512],
                        arena_d[e * CAP + st * 512:e * CAP + (st + 1) * 512,
                                fcc * 128:(fcc + 1) * 128], transpose=True)
            for q in range(4 * NST):
                dma("sync", "sL", wext[:, 2 * q:2 * q + 2],
                    ext_d[e * CAP + q * 128:e * CAP + (q + 1) * 128, :])
            xa_ready = C["sL"]
            C["dv"] += 1
            if on("vector"):
                v.wait_ge(SEM["sL"], xa_ready)
                if e > 0:
                    v.wait_ge(SEM["gS"], prev["scdone"])
                v.tensor_copy(tok_u[:, :],
                              wext[:, :].rearrange("p (q c) -> p q c", c=2)[:, :, 1]) \
                    .then_inc(SEM["dv"], 1)
            tok_ready = C["dv"]

            ac0 = C["ac"]
            gidx = 0
            for icc in range(16):
                for st in range(NST):
                    bank = PB[gidx % 4]
                    C["pe"] += 1
                    if on("tensor"):
                        if gidx == 0:
                            t.wait_ge(SEM["gD"], w_ready)
                            t.wait_ge(SEM["sL"], xa_ready)
                        tgt = ac0 + gidx - 3
                        if tgt > ac0:
                            t.wait_ge(SEM["ac"], tgt)
                        for fcc in range(FC):
                            mm = t.matmul(bank[:, :],
                                          w1T(fcc)[:, icc * 128:(icc + 1) * 128],
                                          ffn_xa(fcc)[:, st * 512:(st + 1) * 512],
                                          start=(fcc == 0), stop=(fcc == FC - 1))
                        mm.then_inc(SEM["pe"], 1)
                    C["ac"] += 1
                    if on("scalar"):
                        s.wait_ge(SEM["pe"], C["pe"])
                        if gidx == 0 and e > 0:
                            s.wait_ge(SEM["pe"], prev["w2free"])
                        s.activation(out=ffn_int(icc)[:, st * 512:(st + 1) * 512],
                                     in_=bank[:, :], func=AF.Relu) \
                            .then_inc(SEM["ac"], 1)
                    gidx += 1
            marks["w1free"] = C["pe"]
            l1_done = C["ac"]

            dv0 = C["dv"]
            gidx = 0
            for q in range(4 * NST):
                st, qq = q // 4, q % 4
                for half in range(2):
                    bank = PB[4 + gidx % 4]
                    C["pe"] += 1
                    if on("tensor"):
                        if gidx == 0:
                            t.wait_ge(SEM["ac"], l1_done)
                            if e > 0:
                                t.wait_ge(SEM["dv"], prev["l2evac"])
                        tgt = dv0 + gidx - 3
                        if tgt > dv0:
                            t.wait_ge(SEM["dv"], tgt)
                        for icc in range(16):
                            mm = t.matmul(bank[:, :],
                                          ffn_int(icc)[:, st * 512 + qq * 128:
                                                       st * 512 + (qq + 1) * 128],
                                          w2T(icc)[:, half * 512:(half + 1) * 512],
                                          start=(icc == 0), stop=(icc == 15))
                        mm.then_inc(SEM["pe"], 1)
                    C["dv"] += 1
                    if on("vector"):
                        v.wait_ge(SEM["pe"], C["pe"])
                        if half == 0 and C["_ffnsc"] >= 32:
                            v.wait_ge(SEM["gS"], C["_ffnsc"] - 16)
                        v.tensor_scalar(osb(q % 2)[:, half * 512:(half + 1) * 512],
                                        bank[:, :], wext[:, 2 * q:2 * q + 1], 0.0,
                                        ALU.mult, ALU.add).then_inc(SEM["dv"], 1)
                    gidx += 1
                C["gS"] += 16
                C["_ffnsc"] = C["gS"]
                if on("gpsimd"):
                    g.wait_ge(SEM["dv"], C["dv"])
                    if q == 0:
                        g.wait_ge(SEM["dv"], tok_ready)
                    g.indirect_dma_start(
                        out=outpad_d[:, :],
                        out_offset=IndirectOffsetOnAxis(tok_u[:, q:q + 1], 0),
                        in_=osb(q % 2), in_offset=None,
                        compute_op=ALU.add, oob_is_err=True,
                    ).then_inc(SEM["gS"], 16)
            marks["w2free"] = C["pe"]
            marks["xafree"] = marks["w1free"]
            marks["l2evac"] = C["dv"]
            marks["scdone"] = C["gS"]
        wait("gpsimd", "gS", C["gS"])
        barrier(eng, E, C)

    def ph_out(eng, E, C):
        dma, on, wait, inc, vop = mk(eng, E, C)
        for gg in range(8):
            dma("sync", "sS", out_t[gg * 1024:(gg + 1) * 1024, :],
                outpad_d[gg * 1024:(gg + 1) * 1024, :])
        wait("sync", "sS", C["sS"])
        barrier(eng, E, C)

    def ph_dbg(eng, E, C):
        dma, on, wait, inc, vop = mk(eng, E, C)
        for i in range(NTILE):
            b = i % 2
            C["sL"] += 16
            if on("sync"):
                if i >= 2:
                    E["sync"].wait_ge(SEM["sS"], C["_dss"] - 16)
                E["sync"].dma_start(out=sc_xt(b), in_=xnbf_d[i * 128:(i + 1) * 128, :]) \
                    .then_inc(SEM["sL"], 16)
            C["ac"] += 1
            if on("scalar"):
                E["scalar"].wait_ge(SEM["sL"], C["sL"])
                if i >= 1:
                    E["scalar"].wait_ge(SEM["sS"], C["_dss"])
                E["scalar"].copy(bs_o(0), sc_xt(b)).then_inc(SEM["ac"], 1)
            C["sS"] += 16
            C["_dss"] = C["sS"]
            if on("sync"):
                E["sync"].wait_ge(SEM["ac"], C["ac"])
                E["sync"].dma_start(out=dbg_xn[i * 128:(i + 1) * 128, :], in_=bs_o(0)) \
                    .then_inc(SEM["sS"], 16)
        dma("sync", "sS", dbg_lg,
            logits_all[:, :, :].rearrange("p t e -> p (t e)"))
        dma("sync", "sS", dbg_m12,
            mask_all[:, :, :].rearrange("p t e -> p (t e)"))
        dma("sync", "sS", dbg_idx, idx_f[:, :])
        dma("sync", "sS", dbg_w, wv_all[:, :])
        dma("sync", "sS", dbg_sm, ln_sm)
        dma("sync", "sS", dbg_d, APF[:, 4608 + 1024:4608 + 2048])
        if full:
            dma("sync", "sS", dbg_kl, kl_all[:, :, :].rearrange("p t e -> p (t e)"))
            dma("sync", "sS", dbg_tabs, tabs[:, :])
            dma("sync", "sS", dbg_rb, rb_all[:, :, :].rearrange("p t e -> p (t e)"))
            dma("sync", "sS", dbg_rk, rank_all[:, :, :].rearrange("p t e -> p (t e)"))
            for gg in range(8):
                dma("sync", "sS", dbg_at[gg * 128:(gg + 1) * 128, :],
                    attnfm_d.ap()[gg * 128:(gg + 1) * 128, :])
            for gg in range(8):
                dma("sync", "sS", dbg_ext[gg * CAP:(gg + 1) * CAP, :],
                    ext_d.ap()[gg * CAP:(gg + 1) * CAP, :])
        wait("sync", "sS", C["sS"])
        barrier(eng, E, C)

    phases = [ph_const, ph_ln, ph_att, ph_top]
    if full:
        phases += [ph_scat, ph_base, ph_ffn, ph_out]
    if MAXPH is not None:
        phases = phases[:MAXPH]
    if DEBUG:
        phases.append(ph_dbg)

    with nc.Block() as block:
        def runner(kind):
            def body(engobj):
                E = {kind: engobj}
                C = {k: 0 for k in sem_names}
                C["_aodma"] = 0
                C["_ffnsc"] = 0
                C["_dss"] = 0
                C["_topdv"] = 0
                for ph in phases:
                    ph(kind, E, C)
            return body
        block.sync(runner("sync"))
        block.tensor(runner("tensor"))
        block.vector(runner("vector"))
        block.scalar(runner("scalar"))
        block.gpsimd(runner("gpsimd"))
    ctx.close()
    return nc


def _host_consts():
    triu = np.triu(np.ones((128, 128), np.float32), 1)
    ident = np.eye(128, dtype=np.float32)
    smat = np.zeros((128, FC * NH), np.float32)
    for fcc in range(FC):
        for dd in range(128):
            h = 2 * fcc + dd // 64
            smat[dd, fcc * NH + h] = 1.0
    emat = np.zeros((16, FC * 128), np.float32)
    for fcc in range(FC):
        for ff in range(128):
            h = 2 * fcc + ff // 64
            emat[h, fcc * 128 + ff] = 1.0
    eidx = np.tile(np.arange(NE, dtype=np.float32)[None, :], (128, 1))
    return (triu, ident, smat.astype(ml_dtypes.bfloat16),
            emat.astype(ml_dtypes.bfloat16), eidx)


def _core_batches(c):
    return [c + 8 * r for r in range(NRUN)]


def kernel(L0_attn_x, L0_attn_Wq, L0_attn_bq, L0_attn_Wk, L0_attn_bk,
           L0_attn_Wv, L0_attn_bv, L0_attn_Wo, L0_attn_bo,
           L0_ffn_expert_embeddings, L0_ffn_first_linear, L0_ffn_second_linear):
    x = np.ascontiguousarray(np.asarray(L0_attn_x, dtype=np.float32))
    wqT = np.ascontiguousarray(np.asarray(L0_attn_Wq).T)
    wkT = np.ascontiguousarray(np.asarray(L0_attn_Wk).T)
    wvT = np.ascontiguousarray(np.asarray(L0_attn_Wv).T)
    woT = np.ascontiguousarray(np.asarray(L0_attn_Wo).T)
    w1t = np.ascontiguousarray(np.transpose(np.asarray(L0_ffn_first_linear), (0, 2, 1)))
    w2t = np.ascontiguousarray(np.transpose(np.asarray(L0_ffn_second_linear), (0, 2, 1)))
    eT = np.ascontiguousarray(np.asarray(L0_ffn_expert_embeddings).T)
    b4 = np.concatenate([np.ascontiguousarray(np.asarray(b).reshape(FC, 128).T)
                         for b in (L0_attn_bq, L0_attn_bk, L0_attn_bv, L0_attn_bo)],
                        axis=1)
    triu, ident, smat, emat, eidx = _host_consts()

    shards = []
    for c in range(NCORE):
        xb = x[_core_batches(c)].reshape(NTOK, D)
        shards.append(np.ascontiguousarray(xb))

    base = dict(wqT=wqT, wkT=wkT, wvT=wvT, woT=woT, w1t=w1t, w2t=w2t, eT=eT,
                bias4=b4, triu=triu, ident=ident, smat=smat, emat=emat, eidx=eidx)

    def sel_mats(c):
        sel = np.zeros((32, 8), np.float32)
        for r in range(NRUN):
            b = c + 8 * r
            sel[b, r] = 1.0
            for rp in range(r):
                sel[c + 8 * rp, 4 + r] = 1.0
        return sel

    nc1 = build_program(full=False)
    in1 = [dict(base, x=shards[c], sel=sel_mats(c)) for c in range(NCORE)]
    res1 = run_bass_kernel_spmd(nc1, in1, core_ids=list(range(NCORE)))
    cnts32 = np.zeros((32, NE), np.float32)
    for c in range(NCORE):
        co = res1.results[c]["cntout"].reshape(NRUN, NE)
        for r in range(NRUN):
            cnts32[c + 8 * r] = co[r]

    nc2 = build_program(full=True)
    in2 = [dict(base, x=shards[c], sel=sel_mats(c), cnts=cnts32) for c in range(NCORE)]
    res2 = run_bass_kernel_spmd(nc2, in2, core_ids=list(range(NCORE)))

    out = np.zeros((32, SEQ, D), np.float32)
    for c in range(NCORE):
        o = res2.results[c]["out"].reshape(NRUN, SEQ, D)
        for r in range(NRUN):
            out[c + 8 * r] = o[r]
    return out
